# version 01
# baseline (speedup 1.0000x reference)
import contextlib
import numpy as np
import ml_dtypes
import concourse.bass as bass
import concourse.mybir as mybir
from concourse.bass_utils import run_bass_kernel_spmd

F32 = mybir.dt.float32
BF16 = mybir.dt.bfloat16
I32 = mybir.dt.int32
AF = mybir.ActivationFunctionType
ALU = mybir.AluOpType
AX = mybir.AxisListType.X
AP = bass.AP

NCORE = 8
T = 4096
D = 1024
NCH = D // 128
TT = 512
NT = T // TT
NHEAD = 16
DH = 64
HID = 4096
NHC = HID // 128
EPS = 1e-5
BF = ml_dtypes.bfloat16


class Ctx:
    def __init__(self, nc, es):
        self.nc = nc
        self.es = es
        self.sems = {}
        self.cnt = {}

    def sem(self, name):
        if name not in self.sems:
            self.sems[name] = self.es.enter_context(self.nc.semaphore(name))
            self.cnt[name] = 0
        return self.sems[name]

    def inc(self, instr, name, n=1):
        instr.then_inc(self.sem(name), n)
        self.cnt[name] += n
        return instr

    def wait(self, eng, name):
        eng.wait_ge(self.sem(name), self.cnt[name])

    def bar(self, eng, instr, name="__bar"):
        self.inc(instr, name, 1)
        eng.wait_ge(self.sem(name), self.cnt[name])


def sbuf(ctx, name, shape, dt):
    return ctx.es.enter_context(ctx.nc.sbuf_tensor(name, shape, dt))


def psum(ctx, name, shape, dt=F32):
    return ctx.es.enter_context(ctx.nc.psum_tensor(name, shape, dt))


def chv(dram_t, t0=None, tn=None):
    v = dram_t.ap().rearrange("(c p) tk -> p c tk", p=128)
    if t0 is not None:
        v = v[:, :, t0:t0 + tn]
    return v



def build_l0():
    nc = bass.Bass("TRN2", target_bir_lowering=False)
    w1 = nc.dram_tensor("w1t", [D, HID], F32, kind="ExternalInput")
    w2 = nc.dram_tensor("w2t", [HID, D], F32, kind="ExternalInput")
    q1 = nc.dram_tensor("wq1s", [D // 8, HID], BF16, kind="ExternalOutput").ap()
    q2 = nc.dram_tensor("wq2s", [HID // 8, D], BF16, kind="ExternalOutput").ap()
    es = contextlib.ExitStack()
    ctx = Ctx(nc, es)
    wt = [sbuf(ctx, f"wt{i}", [128, HID], F32) for i in range(2)]
    acc = sbuf(ctx, "acc", [128, 4], F32)
    onec = sbuf(ctx, "onec", [128, 1], F32)
    oner = sbuf(ctx, "oner", [1, 128], F32)
    stat = sbuf(ctx, "stat", [1, 8], F32)
    mcol = sbuf(ctx, "mcol", [128, 8], F32)
    qt = sbuf(ctx, "qt", [128, HID], BF16)
    tmpa = sbuf(ctx, "tmpa", [128, HID], F32)
    tmpb = sbuf(ctx, "tmpb", [128, HID], F32)
    ps = psum(ctx, "ps", [1, 4], F32)
    psb = psum(ctx, "psb", [128, 8], F32)
    NB1, NB2 = D // 128, HID // 128
    with es, nc.Block() as block:
        @block.sync
        def _(sync):
            for g in range(NB1 + NB2):
                if g >= 2:
                    sync.wait_ge(ctx.sem("freeA"), g - 1)
                if g < NB1:
                    src = w1.ap()[g * 128:(g + 1) * 128, :]
                    dst = wt[g % 2][:]
                else:
                    i = g - NB1
                    src = w2.ap()[i * 128:(i + 1) * 128, :]
                    dst = wt[g % 2][:, 0:D]
                ctx.inc(sync.dma_start(out=dst, in_=src), f"ldA{g % 2}", 16)

        @block.vector
        def _(vector):
            vector.memset(acc[:], 0.0)
            vector.memset(onec[:], 1.0)
            vector.memset(oner[:], 1.0)
            for g in range(NB1 + NB2):
                vector.wait_ge(ctx.sem(f"ldA{g % 2}"), 16 * (g // 2 + 1))
                a, b = (0, 1) if g < NB1 else (2, 3)
                width = HID if g < NB1 else D
                ctx.bar(vector, vector.reduce_sum(acc[:, b:b + 1], wt[g % 2][:, 0:width], AX,
                                                  apply_absolute_value=True))
                ctx.inc(vector.tensor_add(acc[:, a:a + 1], acc[:, a:a + 1], acc[:, b:b + 1]), "freeA", 1)
                ctx.bar(vector, vector.engine_nop())
            ctx.inc(vector.tensor_copy(acc[:, 1:2], acc[:, 2:3]), "acc2", 1)

        @block.tensor
        def _(tensor):
            ctx.wait(tensor, "acc2")
            ctx.inc(tensor.matmul(ps[:, 0:2], onec[:, 0:1], acc[:, 0:2], start=True, stop=True), "summ", 1)

        @block.vector
        def _(vector):
            ctx.wait(vector, "summ")
            for k in (0, 1):
                base = 3 * k
                ctx.bar(vector, vector.tensor_scalar_mul(stat[0:1, base:base + 1], ps[0:1, k:k + 1], 1.0 / (D * HID)))
                ctx.bar(vector, vector.tensor_scalar_max(stat[0:1, base:base + 1], stat[0:1, base:base + 1], 1e-5))
                vector.tensor_scalar_mul(stat[0:1, base + 1:base + 2], stat[0:1, base:base + 1], 0.5)
                ctx.bar(vector, vector.tensor_scalar_mul(stat[0:1, base + 2:base + 3], stat[0:1, base:base + 1], -0.5))
            ctx.inc(vector.engine_nop(), "statdone", 1)

        @block.tensor
        def _(tensor):
            ctx.wait(tensor, "statdone")
            ctx.inc(tensor.matmul(psb[:, 0:6], oner[0:1, :], stat[0:1, 0:6], start=True, stop=True), "bcast", 1)

        @block.vector
        def _(vector):
            ctx.wait(vector, "bcast")
            ctx.inc(vector.tensor_copy(mcol[:, 0:6], psb[:, 0:6]), "mready", 1)

        @block.gpsimd
        def _(g):
            ctx.wait(g, "mready")
            for arm in g.switch_core_id(n=128):
                c = arm.logical % 8
                g.dma_start(out=wt[0][:], in_=w1.ap()[c * 128:(c + 1) * 128, :]).then_inc(ctx.sem("sl1"), 16)
                for bb in range(4):
                    g.dma_start(out=wt[1][:, bb * D:(bb + 1) * D],
                                in_=w2.ap()[c * 512 + bb * 128:c * 512 + (bb + 1) * 128, :]).then_inc(ctx.sem("sl2"), 16)

        @block.vector
        def _(vector):
            vector.wait_ge(ctx.sem("sl1"), 16)
            vector.tensor_scalar(tmpa[:, 0:HID], wt[0][:, 0:HID], mcol[:, 1:2], mcol[:, 0:1], ALU.is_gt, ALU.mult)
            vector.tensor_scalar(tmpb[:, 0:HID], wt[0][:, 0:HID], mcol[:, 2:3], mcol[:, 0:1], ALU.is_lt, ALU.mult)
            ctx.inc(vector.tensor_sub(qt[:, 0:HID], tmpa[:, 0:HID], tmpb[:, 0:HID]), "q1done", 1)
            vector.wait_ge(ctx.sem("sl2"), 64)
            vector.wait_ge(ctx.sem("stq"), 16)
            vector.tensor_scalar(tmpa[:, 0:HID], wt[1][:, 0:HID], mcol[:, 4:5], mcol[:, 3:4], ALU.is_gt, ALU.mult)
            vector.tensor_scalar(tmpb[:, 0:HID], wt[1][:, 0:HID], mcol[:, 5:6], mcol[:, 3:4], ALU.is_lt, ALU.mult)
            ctx.inc(vector.tensor_sub(qt[:, 0:HID], tmpa[:, 0:HID], tmpb[:, 0:HID]), "q2done", 1)

        @block.scalar
        def _(scalar):
            ctx.wait(scalar, "q1done")
            scalar.dma_start(out=q1, in_=qt[:, 0:HID]).then_inc(ctx.sem("stq"), 16)
            ctx.wait(scalar, "q2done")
            scalar.dma_start(out=q2.rearrange("(bb p) d -> p bb d", bb=4),
                             in_=qt[:, 0:HID].rearrange("p (bb d) -> p bb d", bb=4)).then_inc(ctx.sem("stq"), 16)
            scalar.wait_ge(ctx.sem("stq"), 32)
    return nc



def build_l1(stages=99):
    nc = bass.Bass("TRN2", target_bir_lowering=False)
    dram = {}

    def din(name, shape, dt=F32):
        dram[name] = nc.dram_tensor(name, shape, dt, kind="ExternalInput")
        return dram[name]

    def dout(name, shape, dt=BF16):
        dram[name] = nc.dram_tensor(name, shape, dt, kind="ExternalOutput")
        return dram[name]

    def dint(name, shape, dt=BF16):
        dram[name] = nc.dram_tensor(name, shape, dt, kind="Internal")
        return dram[name]

    din("xT", [D, T])
    for nm in ("aWq", "aWk", "aWv", "aWo", "fWr", "fWv", "fWo", "fWs",
               "fW1", "fW2", "fW3", "qWqT", "qWkT", "qWvT"):
        din(nm, [D, D])
    for nm in ("abo", "fln1w", "fln1b", "fln2w", "fln2b", "gln1g", "gln1b", "qbq", "qbk", "qbv", "gb2"):
        din(nm, [128, NCH])
    din("fK", [2, D, HID]); din("fV", [2, HID, D])
    din("fconfW", [D, 8])
    din("fcs", [1, 8])
    din("gwq1T", [D, HID], BF16); din("gwq2T", [HID, D], BF16)
    for nm in ("gb1", "gln2g", "gln2b"):
        din(nm, [128, NHC])

    dout("krev", [D, T]); dout("vout", [D, T]); dout("qout", [D, T])
    dout("x1out", [D, T]); dout("ffn1out", [D, T])
    for nm in ("st_xn", "st_xln", "st_attn", "st_x1mid", "st_state", "st_h",
               "st_xn2", "st_g1", "st_x1"):
        dint(nm, [D, T])
    dint("st_mrows", [8, T])

    es = contextlib.ExitStack()
    ctx = Ctx(nc, es)
    WA = sbuf(ctx, "WA", [128, NCH * D], BF16)
    WB = sbuf(ctx, "WB", [128, NCH * D], BF16)
    WC = sbuf(ctx, "WC", [128, NCH * D], BF16)
    WD = sbuf(ctx, "WD", [128, NCH * D], BF16)
    H8 = sbuf(ctx, "H8", [128, NHC * 1024], BF16)
    E1 = sbuf(ctx, "E1", [128, 4096], BF16)
    E2 = sbuf(ctx, "E2", [128, 4096], BF16)
    E3 = sbuf(ctx, "E3", [128, 4096], BF16)
    E4 = sbuf(ctx, "E4", [128, 4096], BF16)
    E5 = sbuf(ctx, "E5", [128, 4096], BF16)
    E6 = sbuf(ctx, "E6", [128, 4096], BF16)
    SX = sbuf(ctx, "SX", [128, 512], BF16)
    G32 = sbuf(ctx, "G32", [128, 1024], F32)
    rows = sbuf(ctx, "rows", [1, 5 * TT], F32)
    RM, RV, RS, RSTD, RMR, RCP = 0, TT, 2 * TT, 2 * TT, 3 * TT, 4 * TT
    onecb = sbuf(ctx, "onecb", [128, 1], BF16)
    onerf = sbuf(ctx, "onerf", [1, 128], F32)
    onerb = sbuf(ctx, "onerb", [1, 128], BF16)
    colsA = sbuf(ctx, "colsA", [128, 104], F32)
    mstage = sbuf(ctx, "mstage", [1, 1024], BF16)
    MB = sbuf(ctx, "MB", [128, 3 * 1024], BF16)
    Ms = sbuf(ctx, "Ms", [128, 32], F32)
    confW = sbuf(ctx, "confW", [128, 64], BF16)
    iopA = sbuf(ctx, "iopA", [128, 128], I32)
    iopB = sbuf(ctx, "iopB", [128, 128], I32)
    identb = sbuf(ctx, "identb", [128, 128], BF16)
    pbig = psum(ctx, "pbig", [128, 1024], F32)
    pA = psum(ctx, "pA", [128, 512], F32)
    pB = psum(ctx, "pB", [128, 512], F32)
    pC = psum(ctx, "pC", [128, 512], F32)
    prow = psum(ctx, "prow", [1, 1024], F32)
    prtp = psum(ctx, "prtp", [128, 128], BF16)

    blk = [None]

    COL = {"abo": 0, "fln1w": 8, "fln1b": 16, "fln2w": 24, "fln2b": 32,
           "gln1g": 40, "gln1b": 48, "qbq": 56, "qbk": 64, "qbv": 72, "gb2": 96,
           "gb1": 0, "gln2g": 32, "gln2b": 64}

    def w_tile(w, ic, oc):
        return w[:, ic * D + oc * 128: ic * D + (oc + 1) * 128]

    def load_weight(g, name, dest, sem):
        src = dram[name].ap().rearrange("(c p) o -> p c o", p=128)
        ctx.inc(g.dma_start(out=dest[:, 0:NCH * D].rearrange("p (c o) -> p c o", o=D), in_=src), sem, 16)

    def load_cols(g, name, sem, n=NCH):
        ctx.inc(g.dma_start(out=colsA[:, COL[name]:COL[name] + n], in_=dram[name].ap()), sem, 16)

    def stage_init():
        block = blk[0]

        @block.gpsimd
        def _(g):
            g.iota(iopA[:, :], [[1, 128]], channel_multiplier=0)
            ctx.inc(g.iota(iopB[:, :], [[0, 128]], channel_multiplier=1), "iotad", 1)
            load_cols(g, "fln1w", "gwl")
            load_cols(g, "fln1b", "gwl")
            load_cols(g, "fln2w", "gwl")
            load_cols(g, "fln2b", "gwl")
            load_cols(g, "gln1g", "gwl")
            load_cols(g, "gln1b", "gwl")

        @block.vector
        def _(vector):
            vector.memset(onecb[:], 1.0)
            vector.memset(onerf[:], 1.0)
            vector.memset(onerb[:], 1.0)
            ctx.wait(vector, "iotad")
            vector.tensor_tensor(identb[:, :], iopA[:, :], iopB[:, :], ALU.is_equal)
            ctx.inc(vector.engine_nop(), "initd", 1)

    def ln_stage(tag, src_name, dst_name, affine=None, prev=()):
        block = blk[0]
        for t in range(NT):
            @block.gpsimd
            def _(g, t=t):
                if t >= 1:
                    ctx.wait(g, "Lfr")
                else:
                    for p_ in prev:
                        ctx.wait(g, p_)
                ctx.inc(g.dma_start(out=E6.ap().rearrange("p (c tk) -> p c tk", tk=TT),
                                    in_=chv(dram[src_name], t * TT, TT)), "Lld", 16)

            @block.scalar
            def _(scalar, t=t):
                ctx.wait(scalar, "Lld")
                for c in range(NCH):
                    i = scalar.activation(E1[:, c * TT:(c + 1) * TT], E6[:, c * TT:(c + 1) * TT], AF.Square)
                ctx.inc(i, "Lsq", 1)

            @block.tensor
            def _(tensor, t=t):
                ctx.wait(tensor, "Lsq")
                ctx.wait(tensor, "Lld")
                for c in range(NCH):
                    tensor.matmul(prow[0:1, 0:TT], onecb[:, 0:1], E6[:, c * TT:(c + 1) * TT],
                                  start=(c == 0), stop=(c == NCH - 1))
                for c in range(NCH):
                    i = tensor.matmul(prow[0:1, TT:2 * TT], onecb[:, 0:1], E1[:, c * TT:(c + 1) * TT],
                                      start=(c == 0), stop=(c == NCH - 1))
                ctx.inc(i, "Lst", 1)

            @block.vector
            def _(vector, t=t):
                ctx.wait(vector, "Lst")
                vector.tensor_scalar_mul(rows[0:1, RM:RM + TT], prow[0:1, 0:TT], 1.0 / D)
                vector.tensor_scalar_mul(rows[0:1, RV:RV + TT], prow[0:1, TT:2 * TT], 1.0 / D)
                vector.tensor_mul(rows[0:1, RS:RS + TT], rows[0:1, RM:RM + TT], rows[0:1, RM:RM + TT])
                vector.tensor_sub(rows[0:1, RV:RV + TT], rows[0:1, RV:RV + TT], rows[0:1, RS:RS + TT])
                ctx.inc(vector.tensor_scalar_add(rows[0:1, RV:RV + TT], rows[0:1, RV:RV + TT], EPS), "Lvar", 1)

            @block.scalar
            def _(scalar, t=t):
                ctx.wait(scalar, "Lvar")
                scalar.activation(rows[0:1, RS:RS + TT], rows[0:1, RV:RV + TT], AF.Ln)
                ctx.inc(scalar.activation(rows[0:1, RSTD:RSTD + TT], rows[0:1, RS:RS + TT], AF.Exp, scale=-0.5),
                        "Lrs", 1)

            @block.vector
            def _(vector, t=t):
                ctx.wait(vector, "Lrs")
                ctx.inc(vector.tensor_mul(rows[0:1, RMR:RMR + TT], rows[0:1, RM:RM + TT],
                                          rows[0:1, RSTD:RSTD + TT]), "Lmr", 1)

            @block.tensor
            def _(tensor, t=t):
                ctx.wait(tensor, "Lmr")
                tensor.matmul(pbig[:, 0:TT], onerf[0:1, :], rows[0:1, RSTD:RSTD + TT], start=True, stop=True)
                ctx.inc(tensor.matmul(pbig[:, TT:2 * TT], onerf[0:1, :], rows[0:1, RMR:RMR + TT], start=True, stop=True),
                        "Lbc", 1)

            @block.vector
            def _(vector, t=t):
                ctx.wait(vector, "Lbc")
                if affine is not None:
                    ctx.wait(vector, "gwl")
                vector.tensor_copy(G32[:, 0:TT], pbig[:, 0:TT])
                ctx.bar(vector, vector.tensor_copy(G32[:, TT:2 * TT], pbig[:, TT:2 * TT]))
                for c in range(NCH):
                    vector.tensor_mul(E2[:, c * TT:(c + 1) * TT], E6[:, c * TT:(c + 1) * TT], G32[:, 0:TT])
                    if affine is None:
                        i = vector.tensor_sub(E3[:, c * TT:(c + 1) * TT],
                                              E2[:, c * TT:(c + 1) * TT], G32[:, TT:2 * TT])
                    else:
                        vector.tensor_sub(E2[:, c * TT:(c + 1) * TT],
                                          E2[:, c * TT:(c + 1) * TT], G32[:, TT:2 * TT])
                        wc, bc = COL[affine[0]], COL[affine[1]]
                        i = vector.tensor_scalar(E3[:, c * TT:(c + 1) * TT], E2[:, c * TT:(c + 1) * TT],
                                                 colsA[:, wc + c:wc + c + 1], colsA[:, bc + c:bc + c + 1],
                                                 ALU.mult, ALU.add)
                ctx.inc(i, "Lap", 1)

            @block.scalar
            def _(scalar, t=t):
                ctx.wait(scalar, "Lap")
                ctx.inc(scalar.dma_start(out=chv(dram[dst_name], t * TT, TT),
                                         in_=E3.ap().rearrange("p (c tk) -> p c tk", tk=TT)),
                        "Lfr", 16)

    def stage_attn():
        block = blk[0]

        @block.gpsimd
        def _(g):
            load_weight(g, "aWq", WA, "gwl")
            load_weight(g, "aWk", WB, "gwl")
            load_weight(g, "aWv", WC, "gwl")
            load_weight(g, "aWo", WD, "gwl")
            load_cols(g, "abo", "gwl")

        NWG = T // 512
        for wg in range(NWG):
            w0 = wg * 512

            @block.sync
            def _(sync, wg=wg, w0=w0):
                if wg >= 1:
                    ctx.wait(sync, "atst")
                else:
                    ctx.wait(sync, "Lfr")
                ctx.inc(sync.dma_start(out=E5.ap().rearrange("p (c tk) -> p c tk", tk=512),
                                       in_=chv(dram["st_xn"], w0, 512)), "sld", 16)

            for kk, wsb in ((0, WA), (1, WB)):
                dstE = E1 if kk == 0 else E2
                dstp = pA if kk == 0 else pB
                for oc in range(NCH):
                    @block.tensor
                    def _(tensor, wg=wg, kk=kk, wsb=wsb, dstp=dstp, oc=oc):
                        if wg == 0 and kk == 0 and oc == 0:
                            ctx.wait(tensor, "gwl")
                            ctx.wait(tensor, "initd")
                        ctx.wait(tensor, "sld")
                        if not (kk == 0 and oc == 0):
                            ctx.wait(tensor, "qkev")
                        for ic in range(NCH):
                            i = tensor.matmul(dstp[:, 0:512], w_tile(wsb, ic, oc),
                                              E5[:, ic * 512:(ic + 1) * 512],
                                              start=(ic == 0), stop=(ic == NCH - 1))
                        ctx.inc(i, "qk", 1)

                    @block.vector
                    def _(vector, dstE=dstE, dstp=dstp, oc=oc):
                        ctx.wait(vector, "qk")
                        ctx.inc(vector.tensor_copy(dstE[:, oc * 512:(oc + 1) * 512], dstp[:, 0:512]), "qkev", 1)

            for tg in range(4):
                @block.tensor
                def _(tensor, wg=wg, tg=tg):
                    if tg != 0:
                        ctx.wait(tensor, "vev")
                    for ic in range(NCH):
                        for hf in range(2):
                            i = tensor.matmul(pbig[:, hf * 512:(hf + 1) * 512],
                                              E5[:, ic * 512 + tg * 128: ic * 512 + (tg + 1) * 128],
                                              WC[:, ic * D + hf * 512: ic * D + (hf + 1) * 512],
                                              start=(ic == 0), stop=(ic == NCH - 1))
                    ctx.inc(i, "vmm", 1)

                @block.scalar
                def _(scalar, tg=tg):
                    ctx.wait(scalar, "vmm")
                    ctx.inc(scalar.activation(E3[:, tg * 1024:(tg + 1) * 1024], pbig[:, 0:1024], AF.Identity),
                            "vev", 1)

            for w in range(2):
                for h in range(NHEAD):
                    c, b = h // 2, 64 * (h % 2)

                    @block.tensor
                    def _(tensor, wg=wg, w=w, h=h, c=c, b=b):
                        if w == 0 and h == 0:
                            ctx.wait(tensor, "vev")
                            ctx.wait(tensor, "qkev")
                        else:
                            ctx.wait(tensor, "oev")
                        for jt in range(2):
                            i = tensor.matmul(
                                pC[:, jt * 256:(jt + 1) * 256],
                                E2[b:b + DH, c * 512 + w * 256 + jt * 128: c * 512 + w * 256 + (jt + 1) * 128],
                                E1[b:b + DH, c * 512 + w * 256: c * 512 + (w + 1) * 256],
                                start=True, stop=True)
                        ctx.inc(i, "sm", 1)

                    @block.scalar
                    def _(scalar, w=w, h=h):
                        ctx.wait(scalar, "sm")
                        ctx.inc(scalar.activation(SX[:, 0:512], pC[:, 0:512], AF.Exp, scale=0.125), "sx", 1)

                    @block.tensor
                    def _(tensor, w=w, h=h, c=c, b=b):
                        ctx.wait(tensor, "sx")
                        for kt in range(2):
                            i = tensor.matmul(pbig[0:DH, 0:256],
                                              E3[:, (2 * w + kt) * 1024 + 64 * h: (2 * w + kt) * 1024 + 64 * h + DH],
                                              SX[:, kt * 256:(kt + 1) * 256],
                                              start=(kt == 0), stop=(kt == 1))
                        ctx.inc(i, "av", 1)
                        for kt in range(2):
                            i = tensor.matmul(prow[0:1, 0:256], onecb[:, 0:1],
                                              SX[:, kt * 256:(kt + 1) * 256],
                                              start=(kt == 0), stop=(kt == 1))
                        ctx.inc(i, "dn", 1)

                    @block.vector
                    def _(vector, w=w, h=h):
                        ctx.wait(vector, "dn")
                        ctx.bar(vector, vector.reciprocal(rows[0:1, RCP:RCP + 256], prow[0:1, 0:256]))
                        ctx.inc(vector.engine_nop(), "rc", 1)

                    @block.tensor
                    def _(tensor, w=w, h=h):
                        ctx.wait(tensor, "rc")
                        ctx.inc(tensor.matmul(pbig[0:DH, 256:512], onerf[0:1, 0:DH], rows[0:1, RCP:RCP + 256],
                                              start=True, stop=True), "rb", 1)

                    @block.vector
                    def _(vector, w=w, h=h, c=c, b=b):
                        ctx.wait(vector, "rb")
                        ctx.wait(vector, "av")
                        ctx.bar(vector, vector.tensor_copy(G32[0:DH, 0:256], pbig[0:DH, 256:512]))
                        ctx.inc(vector.tensor_mul(E4[b:b + DH, c * 512 + w * 256: c * 512 + (w + 1) * 256],
                                                  pbig[0:DH, 0:256], G32[0:DH, 0:256]),
                                "oev", 1)

            for oc in range(NCH):
                @block.tensor
                def _(tensor, wg=wg, oc=oc):
                    if oc == 0:
                        ctx.wait(tensor, "oev")
                    else:
                        ctx.wait(tensor, "woev")
                    for ic in range(NCH):
                        i = tensor.matmul(pA[:, 0:512], w_tile(WD, ic, oc),
                                          E4[:, ic * 512:(ic + 1) * 512],
                                          start=(ic == 0), stop=(ic == NCH - 1))
                    ctx.inc(i, "wo", 1)

                @block.scalar
                def _(scalar, oc=oc):
                    ctx.wait(scalar, "wo")
                    ctx.inc(scalar.activation(E6[:, oc * 512:(oc + 1) * 512], pA[:, 0:512],
                                              AF.Identity, bias=colsA[:, COL["abo"] + oc:COL["abo"] + oc + 1]),
                            "woev", 1)

            @block.scalar
            def _(scalar, wg=wg, w0=w0):
                ctx.wait(scalar, "woev")
                ctx.inc(scalar.dma_start(out=chv(dram["st_attn"], w0, 512),
                                         in_=E6.ap().rearrange("p (c tk) -> p c tk", tk=512)),
                        "atst", 16)

    def stage_ffn_front():
        block = blk[0]

        @block.gpsimd
        def _(g):
            ctx.wait(g, "atst")
            load_weight(g, "fWr", WA, "gwl")
            load_weight(g, "fWv", WB, "gwl")
            load_weight(g, "fWo", WC, "gwl")
            load_weight(g, "fWs", WD, "gwl")

        for t in range(NT):
            t0 = t * TT

            @block.gpsimd
            def _(g, t=t, t0=t0):
                if t >= 1:
                    ctx.wait(g, "ffst")
                else:
                    ctx.wait(g, "Lfr")
                ctx.inc(g.dma_start(out=E6.ap().rearrange("p (c tk) -> p c tk", tk=TT),
                                    in_=chv(dram["st_xln"], t0, TT)), "gact", 16)
                ctx.inc(g.dma_start(out=E5.ap().rearrange("p (c tk) -> p c tk", tk=TT),
                                    in_=chv(dram["st_xn"], t0, TT)), "gact", 16)

            for oc in range(NCH):
                @block.tensor
                def _(tensor, t=t, oc=oc):
                    ctx.wait(tensor, "gact")
                    if t == 0 and oc == 0:
                        ctx.wait(tensor, "gwl")
                    if not (t == 0 and oc == 0):
                        ctx.wait(tensor, "ffgu")
                    for ic in range(NCH):
                        tensor.matmul(pA[:, 0:TT], w_tile(WA, ic, oc), E6[:, ic * TT:(ic + 1) * TT],
                                      start=(ic == 0), stop=(ic == NCH - 1))
                    for ic in range(NCH):
                        i = tensor.matmul(pB[:, 0:TT], w_tile(WB, ic, oc), E6[:, ic * TT:(ic + 1) * TT],
                                          start=(ic == 0), stop=(ic == NCH - 1))
                    ctx.inc(i, "ffmm", 1)

                @block.scalar
                def _(scalar, oc=oc):
                    ctx.wait(scalar, "ffmm")
                    ctx.inc(scalar.activation(E1[:, oc * TT:(oc + 1) * TT], pA[:, 0:TT], AF.Sigmoid), "ffsig", 1)

                @block.vector
                def _(vector, oc=oc):
                    ctx.wait(vector, "ffmm")
                    ctx.wait(vector, "ffsig")
                    vector.tensor_copy(E2[:, oc * TT:(oc + 1) * TT], pB[:, 0:TT])
                    ctx.inc(vector.tensor_mul(E3[:, oc * TT:(oc + 1) * TT],
                                              E1[:, oc * TT:(oc + 1) * TT],
                                              E2[:, oc * TT:(oc + 1) * TT]), "ffgu", 1)

            for oc in range(NCH):
                @block.tensor
                def _(tensor, oc=oc):
                    ctx.wait(tensor, "ffgu")
                    if oc != 0:
                        ctx.wait(tensor, "ffx1")
                        ctx.wait(tensor, "ffsst")
                    for ic in range(NCH):
                        tensor.matmul(pA[:, 0:TT], w_tile(WC, ic, oc), E3[:, ic * TT:(ic + 1) * TT],
                                      start=(ic == 0), stop=(ic == NCH - 1))
                    for ic in range(NCH):
                        i = tensor.matmul(pB[:, 0:TT], w_tile(WD, ic, oc), E6[:, ic * TT:(ic + 1) * TT],
                                          start=(ic == 0), stop=(ic == NCH - 1))
                    ctx.inc(i, "ffmm2", 1)

                @block.vector
                def _(vector, oc=oc):
                    ctx.wait(vector, "ffmm2")
                    ctx.inc(vector.tensor_add(E1[:, oc * TT:(oc + 1) * TT],
                                              pA[:, 0:TT], E5[:, oc * TT:(oc + 1) * TT]), "ffx1", 1)

                @block.scalar
                def _(scalar, oc=oc):
                    ctx.wait(scalar, "ffmm2")
                    ctx.inc(scalar.activation(E2[:, oc * TT:(oc + 1) * TT], pB[:, 0:TT], AF.Identity), "ffsst", 1)

            @block.scalar
            def _(scalar, t0=t0):
                ctx.wait(scalar, "ffx1")
                ctx.wait(scalar, "ffsst")
                ctx.inc(scalar.dma_start(out=chv(dram["st_x1mid"], t0, TT),
                                         in_=E1.ap().rearrange("p (c tk) -> p c tk", tk=TT)), "ffst", 16)
                ctx.inc(scalar.dma_start(out=chv(dram["st_state"], t0, TT),
                                         in_=E2.ap().rearrange("p (c tk) -> p c tk", tk=TT)), "ffst", 16)

    def stage_bids():
        block = blk[0]

        @block.gpsimd
        def _(g):
            ctx.wait(g, "Lap")
            src = dram["fconfW"].ap().rearrange("(c p) o -> p c o", p=128)
            ctx.inc(g.dma_start(out=confW.ap().rearrange("p (c o) -> p c o", o=8), in_=src), "gwl", 16)
            ctx.inc(g.dma_start(out=rows[0:1, RCP:RCP + 8], in_=dram["fcs"].ap()), "gwl", 16)

        @block.tensor
        def _(tensor):
            ctx.wait(tensor, "gwl")
            ctx.inc(tensor.matmul(pbig[:, 0:8], onerf[0:1, :], rows[0:1, RCP:RCP + 8], start=True, stop=True),
                    "bcsb", 1)

        @block.vector
        def _(vector):
            ctx.wait(vector, "bcsb")
            ctx.inc(vector.tensor_copy(Ms[:, 24:32], pbig[:, 0:8]), "bcsv", 1)

        for t in range(NT):
            t0 = t * TT

            @block.gpsimd
            def _(g, t=t, t0=t0):
                if t >= 1:
                    ctx.wait(g, "bmask")
                else:
                    ctx.wait(g, "Lfr")
                ctx.inc(g.dma_start(out=E6.ap().rearrange("p (c tk) -> p c tk", tk=TT),
                                    in_=chv(dram["st_h"], t0, TT)), "gact", 16)

            for g4 in range(4):
                gg = t * 4 + g4

                @block.tensor
                def _(tensor, t=t, g4=g4):
                    ctx.wait(tensor, "gact")
                    if t == 0 and g4 == 0:
                        ctx.wait(tensor, "bcsv")
                    else:
                        ctx.wait(tensor, "bmask")
                    for ic in range(NCH):
                        i = tensor.matmul(pA[:, 0:8], E6[:, ic * TT + g4 * 128: ic * TT + (g4 + 1) * 128],
                                          confW[:, ic * 8:(ic + 1) * 8],
                                          start=(ic == 0), stop=(ic == NCH - 1))
                    ctx.inc(i, "bmm", 1)

                @block.scalar
                def _(scalar):
                    ctx.wait(scalar, "bmm")
                    ctx.inc(scalar.activation(Ms[:, 0:4], pA[:, 0:4], AF.Sigmoid), "bsig", 1)

                @block.vector
                def _(vector, gg=gg):
                    ctx.wait(vector, "bsig")
                    ctx.bar(vector, vector.tensor_copy(Ms[:, 4:7], pA[:, 4:7]))
                    vector.tensor_mul(Ms[:, 8:11], Ms[:, 0:3], Ms[:, 24:27])
                    ctx.bar(vector, vector.engine_nop())
                    vector.tensor_scalar(Ms[:, 8:11], Ms[:, 8:11], Ms[:, 3:4], None, ALU.mult)
                    ctx.bar(vector, vector.engine_nop())
                    ctx.bar(vector, vector.tensor_add(Ms[:, 8:11], Ms[:, 8:11], Ms[:, 4:7]))
                    vector.tensor_tensor(Ms[:, 11:12], Ms[:, 8:9], Ms[:, 9:10], ALU.is_ge)
                    vector.tensor_tensor(Ms[:, 12:13], Ms[:, 8:9], Ms[:, 10:11], ALU.is_ge)
                    vector.tensor_tensor(Ms[:, 13:14], Ms[:, 9:10], Ms[:, 8:9], ALU.is_gt)
                    vector.tensor_tensor(Ms[:, 14:15], Ms[:, 9:10], Ms[:, 10:11], ALU.is_ge)
                    vector.tensor_tensor(Ms[:, 15:16], Ms[:, 10:11], Ms[:, 8:9], ALU.is_gt)
                    ctx.bar(vector, vector.tensor_tensor(Ms[:, 16:17], Ms[:, 10:11], Ms[:, 9:10], ALU.is_gt))
                    vector.tensor_mul(Ms[:, 17:18], Ms[:, 11:12], Ms[:, 12:13])
                    vector.tensor_mul(Ms[:, 18:19], Ms[:, 13:14], Ms[:, 14:15])
                    ctx.bar(vector, vector.tensor_mul(Ms[:, 19:20], Ms[:, 15:16], Ms[:, 16:17]))
                    vector.tensor_mul(Ms[:, 20:21], Ms[:, 17:18], Ms[:, 0:1])
                    vector.tensor_mul(Ms[:, 21:22], Ms[:, 18:19], Ms[:, 1:2])
                    ctx.bar(vector, vector.tensor_mul(Ms[:, 22:23], Ms[:, 19:20], Ms[:, 2:3]))
                    vector.tensor_add(Ms[:, 20:21], Ms[:, 20:21], Ms[:, 21:22])
                    ctx.bar(vector, vector.engine_nop())
                    ctx.bar(vector, vector.tensor_add(Ms[:, 20:21], Ms[:, 20:21], Ms[:, 22:23]))
                    ctx.bar(vector, vector.tensor_scalar_add(Ms[:, 21:22], Ms[:, 20:21], 1e-6))
                    ctx.bar(vector, vector.reciprocal(Ms[:, 21:22], Ms[:, 21:22]))
                    ctx.bar(vector, vector.tensor_mul(Ms[:, 21:22], Ms[:, 20:21], Ms[:, 21:22]))
                    for e in range(3):
                        vector.tensor_scalar(E4[:, e * 32 + gg: e * 32 + gg + 1], Ms[:, 17 + e:18 + e],
                                             Ms[:, 21:22], None, ALU.mult)
                    ctx.inc(vector.engine_nop(), "bmask", 1)

        @block.tensor
        def _(tensor):
            ctx.wait(tensor, "bmask")
            ctx.inc(tensor.transpose(prtp[:, :], E4[:, 0:128], identb[:, :]), "btp", 1)

        @block.vector
        def _(vector):
            ctx.wait(vector, "btp")
            ctx.inc(vector.tensor_copy(E1[:, 0:128], prtp[:, :]), "btv", 1)

        @block.scalar
        def _(scalar):
            ctx.wait(scalar, "btv")
            for e in range(3):
                ctx.inc(scalar.dma_start(out=dram["st_mrows"].ap()[e, :].rearrange("(g q) -> g q", q=128),
                                         in_=E1[32 * e:32 * e + 32, 0:128]), "bmst", 16)

    def stage_experts():
        block = blk[0]

        @block.gpsimd
        def _(g):
            ctx.wait(g, "bmst")
            load_weight(g, "fW1", WC, "gwl")
            load_weight(g, "fW2", WD, "gwl")
            src = dram["fW3"].ap().rearrange("(c p) o -> p c o", p=128)
            ctx.inc(g.dma_start(out=E1.ap().rearrange("p (c o) -> p c o", o=D)[:, 0:4, :],
                                in_=src[:, 0:4, :]), "gwl", 16)
            ctx.inc(g.dma_start(out=E2.ap().rearrange("p (c o) -> p c o", o=D)[:, 0:4, :],
                                in_=src[:, 4:8, :]), "gwl", 16)

        def w3_tile(ic, oc):
            E = E1 if ic < 4 else E2
            icc = ic % 4
            return E[:, icc * D + oc * 128: icc * D + (oc + 1) * 128]

        def hofs(ic):
            EH = E5 if ic < 4 else E6
            return EH, (ic % 4) * 1024

        NQ = T // 1024
        for q in range(NQ):
            q0 = q * 1024

            @block.gpsimd
            def _(g, q=q, q0=q0):
                if q >= 1:
                    ctx.wait(g, "xst")
                else:
                    ctx.wait(g, "bmask")
                ctx.inc(g.dma_start(out=E5.ap().rearrange("p (c tk) -> p c tk", tk=1024),
                                    in_=chv(dram["st_h"], q0, 1024)[:, 0:4, :]), "gact", 16)
                ctx.inc(g.dma_start(out=E6.ap().rearrange("p (c tk) -> p c tk", tk=1024),
                                    in_=chv(dram["st_h"], q0, 1024)[:, 4:8, :]), "gact", 16)


            for e in range(3):
                @block.gpsimd
                def _(g, e=e, q=q, q0=q0):
                    if not (q == 0 and e == 0):
                        ctx.wait(g, "xmb")
                    ctx.inc(g.dma_start(out=mstage[0:1, 0:1024],
                                        in_=dram["st_mrows"].ap()[e:e + 1, q0:q0 + 1024]), "gact", 16)

                @block.tensor
                def _(tensor, e=e, q=q):
                    ctx.wait(tensor, "gact")
                    if q == 0 and e == 0:
                        ctx.wait(tensor, "gwl")
                    if not (q == 0 and e == 0):
                        ctx.wait(tensor, "xmbv")
                    for hf in range(2):
                        i = tensor.matmul(pbig[:, hf * 512:(hf + 1) * 512], onerb[0:1, :],
                                          mstage[0:1, hf * 512:(hf + 1) * 512],
                                          start=True, stop=True)
                    ctx.inc(i, "xmb", 1)

                @block.vector
                def _(vector, e=e):
                    ctx.wait(vector, "xmb")
                    ctx.inc(vector.tensor_copy(MB[:, e * 1024:(e + 1) * 1024], pbig[:, 0:1024]), "xmbv", 1)

            for e in range(2):
                for hc in range(NHC):
                    @block.gpsimd
                    def _(g, e=e, hc=hc, q=q):
                        par = hc % 2
                        if hc >= 2 or q > 0 or e > 0:
                            ctx.wait(g, "pfree")
                        src = dram["fK"].ap()[e, :, hc * 128:(hc + 1) * 128].rearrange("(c p) o -> p c o", p=128)
                        ctx.inc(g.dma_start(out=WA[:, par * 4096: par * 4096 + NCH * 128]
                                            .rearrange("p (c o) -> p c o", o=128), in_=src), "gact", 16)

                    @block.tensor
                    def _(tensor, e=e, hc=hc, q=q):
                        par = hc % 2
                        ctx.wait(tensor, "gact")
                        ctx.wait(tensor, "gact")
                        ctx.wait(tensor, "xmbv")
                        if hc >= 1 or e > 0 or q > 0:
                            ctx.wait(tensor, "xhrev")
                        for ic in range(NCH):
                            EH, off = hofs(ic)
                            for hf in range(2):
                                i = tensor.matmul(pbig[:, hf * 512:(hf + 1) * 512],
                                                  WA[:, par * 4096 + ic * 128: par * 4096 + (ic + 1) * 128],
                                                  EH[:, off + hf * 512: off + (hf + 1) * 512],
                                                  start=(ic == 0), stop=(ic == NCH - 1))
                        ctx.inc(i, "pfree", 1)

                    @block.scalar
                    def _(scalar, hc=hc):
                        ctx.wait(scalar, "pfree")
                        ctx.inc(scalar.activation(H8[:, hc * 1024:(hc + 1) * 1024], pbig[:, 0:1024], AF.Relu),
                                "xrelu", 1)

                    @block.vector
                    def _(vector, e=e, hc=hc):
                        ctx.wait(vector, "xrelu")
                        vector.tensor_mul(H8[:, hc * 1024:(hc + 1) * 1024],
                                          H8[:, hc * 1024:(hc + 1) * 1024],
                                          H8[:, hc * 1024:(hc + 1) * 1024])
                        ctx.inc(vector.tensor_mul(H8[:, hc * 1024:(hc + 1) * 1024],
                                                  H8[:, hc * 1024:(hc + 1) * 1024],
                                                  MB[:, e * 1024:(e + 1) * 1024]), "xhrev", 1)

                for oc in range(NCH):
                    @block.gpsimd
                    def _(g, e=e, oc=oc, q=q):
                        par = oc % 2
                        if oc >= 2 or e > 0 or q > 0:
                            ctx.wait(g, "pfree")
                        src = dram["fV"].ap()[e, :, oc * 128:(oc + 1) * 128].rearrange("(c p) o -> p c o", p=128)
                        ctx.inc(g.dma_start(out=WB[:, par * 4096: par * 4096 + NHC * 128]
                                            .rearrange("p (c o) -> p c o", o=128), in_=src), "gact", 16)

                    @block.tensor
                    def _(tensor, e=e, oc=oc, q=q):
                        par = oc % 2
                        ctx.wait(tensor, "gact")
                        ctx.wait(tensor, "xhrev")
                        if oc >= 1 or e > 0 or q > 0:
                            ctx.wait(tensor, "xvev")
                        for t2 in range(2):
                            dstp = pA if t2 == 0 else pB
                            for hc in range(NHC):
                                i = tensor.matmul(dstp[:, 0:512],
                                                  WB[:, par * 4096 + hc * 128: par * 4096 + (hc + 1) * 128],
                                                  H8[:, hc * 1024 + t2 * 512: hc * 1024 + (t2 + 1) * 512],
                                                  start=(hc == 0), stop=(hc == NHC - 1))
                        ctx.inc(i, "pfree", 1)

                    @block.vector
                    def _(vector, e=e, oc=oc):
                        ctx.wait(vector, "pfree")
                        EO = E3 if oc < 4 else E4
                        off = (oc % 4) * 1024
                        if e == 0:
                            vector.tensor_copy(EO[:, off:off + 512], pA[:, 0:512])
                            i = vector.tensor_copy(EO[:, off + 512:off + 1024], pB[:, 0:512])
                        else:
                            vector.tensor_add(EO[:, off:off + 512], EO[:, off:off + 512], pA[:, 0:512])
                            i = vector.tensor_add(EO[:, off + 512:off + 1024], EO[:, off + 512:off + 1024],
                                                  pB[:, 0:512])
                        ctx.inc(i, "xvev", 1)

            @block.gpsimd
            def _(g, q=q, q0=q0):
                ctx.wait(g, "xvev")
                ctx.inc(g.dma_start(out=H8.ap().rearrange("p (c tk) -> p c tk", tk=1024)[:, 16:24, :],
                                    in_=chv(dram["st_state"], q0, 1024)), "gact", 16)

            for oc in range(NCH):
                @block.tensor
                def _(tensor, oc=oc):
                    ctx.wait(tensor, "gact")
                    if oc != 0:
                        ctx.wait(tensor, "xtev")
                    for ic in range(NCH):
                        EH, off = hofs(ic)
                        tensor.matmul(pA[:, 0:512], w_tile(WC, ic, oc), EH[:, off:off + 512],
                                      start=(ic == 0), stop=(ic == NCH - 1))
                        tensor.matmul(pB[:, 0:512], w_tile(WC, ic, oc), EH[:, off + 512:off + 1024],
                                      start=(ic == 0), stop=(ic == NCH - 1))
                    for ic in range(NCH):
                        tensor.matmul(pC[:, 0:512], w_tile(WD, ic, oc),
                                      H8[:, (16 + ic) * 1024: (16 + ic) * 1024 + 512],
                                      start=(ic == 0), stop=(ic == NCH - 1))
                        i = tensor.matmul(pbig[:, 0:512], w_tile(WD, ic, oc),
                                          H8[:, (16 + ic) * 1024 + 512: (17 + ic) * 1024],
                                          start=(ic == 0), stop=(ic == NCH - 1))
                    ctx.inc(i, "xtmm", 1)

                @block.scalar
                def _(scalar, oc=oc):
                    ctx.wait(scalar, "xtmm")
                    scalar.activation(H8[:, (8 + oc) * 1024: (8 + oc) * 1024 + 512], pC[:, 0:512], AF.Sigmoid)
                    ctx.inc(scalar.activation(H8[:, (8 + oc) * 1024 + 512: (9 + oc) * 1024],
                                              pbig[:, 0:512], AF.Sigmoid), "xtsig", 1)

                @block.vector
                def _(vector, oc=oc):
                    ctx.wait(vector, "xtmm")
                    ctx.wait(vector, "xtsig")
                    vector.tensor_copy(H8[:, oc * 1024: oc * 1024 + 512], pA[:, 0:512])
                    vector.tensor_copy(H8[:, oc * 1024 + 512:(oc + 1) * 1024], pB[:, 0:512])
                    vector.tensor_mul(H8[:, oc * 1024:(oc + 1) * 1024],
                                      H8[:, oc * 1024:(oc + 1) * 1024],
                                      H8[:, (8 + oc) * 1024:(9 + oc) * 1024])
                    ctx.inc(vector.tensor_mul(H8[:, oc * 1024:(oc + 1) * 1024],
                                              H8[:, oc * 1024:(oc + 1) * 1024],
                                              MB[:, 2 * 1024:3 * 1024]), "xtev", 1)

            for oc in range(NCH):
                @block.tensor
                def _(tensor, oc=oc):
                    ctx.wait(tensor, "xtev")
                    if oc != 0:
                        ctx.wait(tensor, "xw3ev")
                    for t2 in range(2):
                        dstp = pA if t2 == 0 else pB
                        for ic in range(NCH):
                            i = tensor.matmul(dstp[:, 0:512], w3_tile(ic, oc),
                                              H8[:, ic * 1024 + t2 * 512: ic * 1024 + (t2 + 1) * 512],
                                              start=(ic == 0), stop=(ic == NCH - 1))
                    ctx.inc(i, "xw3", 1)

                @block.vector
                def _(vector, oc=oc):
                    ctx.wait(vector, "xw3")
                    EO = E3 if oc < 4 else E4
                    off = (oc % 4) * 1024
                    vector.tensor_add(EO[:, off:off + 512], EO[:, off:off + 512], pA[:, 0:512])
                    ctx.inc(vector.tensor_add(EO[:, off + 512:off + 1024], EO[:, off + 512:off + 1024],
                                              pB[:, 0:512]), "xw3ev", 1)

            @block.gpsimd
            def _(g, q0=q0):
                ctx.wait(g, "xtev")
                ctx.inc(g.dma_start(out=E5.ap().rearrange("p (c tk) -> p c tk", tk=1024),
                                    in_=chv(dram["st_attn"], q0, 1024)[:, 0:4, :]), "gact", 16)
                ctx.inc(g.dma_start(out=E6.ap().rearrange("p (c tk) -> p c tk", tk=1024),
                                    in_=chv(dram["st_attn"], q0, 1024)[:, 4:8, :]), "gact", 16)
                ctx.inc(g.dma_start(out=H8.ap().rearrange("p (c tk) -> p c tk", tk=1024)[:, 24:32, :],
                                    in_=chv(dram["xT"], q0, 1024)), "gact", 16)
                ctx.inc(g.dma_start(out=H8.ap().rearrange("p (c tk) -> p c tk", tk=1024)[:, 16:24, :],
                                    in_=chv(dram["st_x1mid"], q0, 1024)), "gact", 16)

            @block.vector
            def _(vector, q=q):
                ctx.wait(vector, "xw3ev")
                ctx.wait(vector, "gact")
                ctx.wait(vector, "gact")
                ctx.wait(vector, "gact")
                for ic in range(NCH):
                    EH = E5 if ic < 4 else E6
                    off = (ic % 4) * 1024
                    EO = E3 if ic < 4 else E4
                    vector.tensor_add(EO[:, off:off + 1024], EO[:, off:off + 1024],
                                      H8[:, (16 + ic) * 1024:(17 + ic) * 1024])
                    vector.tensor_add(EO[:, off:off + 1024], EO[:, off:off + 1024], EH[:, off:off + 1024])
                    i = vector.tensor_add(EO[:, off:off + 1024], EO[:, off:off + 1024],
                                          H8[:, (24 + ic) * 1024:(25 + ic) * 1024])
                ctx.inc(i, "xasm", 1)

            @block.scalar
            def _(scalar, q0=q0):
                ctx.wait(scalar, "xasm")
                ctx.inc(scalar.dma_start(out=chv(dram["x1out"], q0, 1024)[:, 0:4, :],
                                         in_=E3.ap().rearrange("p (c tk) -> p c tk", tk=1024)), "xst", 16)
                ctx.inc(scalar.dma_start(out=chv(dram["x1out"], q0, 1024)[:, 4:8, :],
                                         in_=E4.ap().rearrange("p (c tk) -> p c tk", tk=1024)), "xst", 16)
                ctx.inc(scalar.dma_start(out=chv(dram["st_x1"], q0, 1024)[:, 0:4, :],
                                         in_=E3.ap().rearrange("p (c tk) -> p c tk", tk=1024)), "xst", 16)
                ctx.inc(scalar.dma_start(out=chv(dram["st_x1"], q0, 1024)[:, 4:8, :],
                                         in_=E4.ap().rearrange("p (c tk) -> p c tk", tk=1024)), "xst", 16)

    def stage_qkv():
        block = blk[0]

        @block.gpsimd
        def _(g):
            ctx.wait(g, "x2ap")
            load_weight(g, "qWqT", WA, "gwl")
            load_weight(g, "qWkT", WB, "gwl")
            load_weight(g, "qWvT", WC, "gwl")
            load_cols(g, "qbq", "gwl")
            load_cols(g, "qbk", "gwl")
            load_cols(g, "qbv", "gwl")

        for t in range(NT):
            t0 = t * TT

            @block.gpsimd
            def _(g, t=t, t0=t0):
                if t >= 1:
                    ctx.wait(g, "qst")
                else:
                    ctx.wait(g, "Lfr")
                ctx.inc(g.dma_start(out=E6.ap().rearrange("p (c tk) -> p c tk", tk=TT),
                                    in_=chv(dram["st_xn2"], t0, TT)), "gact", 16)

            for kk, wsb, bkey in ((0, WA, "qbq"), (1, WB, "qbk"), (2, WC, "qbv")):
                for oc in range(NCH):
                    @block.tensor
                    def _(tensor, t=t, kk=kk, wsb=wsb, oc=oc):
                        ctx.wait(tensor, "gact")
                        if t == 0 and kk == 0 and oc == 0:
                            ctx.wait(tensor, "gwl")
                        if not (kk == 0 and oc == 0):
                            ctx.wait(tensor, "qev")
                        for ic in range(NCH):
                            i = tensor.matmul(pA[:, 0:TT], w_tile(wsb, ic, oc), E6[:, ic * TT:(ic + 1) * TT],
                                              start=(ic == 0), stop=(ic == NCH - 1))
                        ctx.inc(i, "qmm", 1)

                    @block.scalar
                    def _(scalar, kk=kk, oc=oc, bkey=bkey):
                        ctx.wait(scalar, "qmm")
                        dstE = (E1, E2, E3)[kk]
                        ctx.inc(scalar.activation(dstE[:, oc * TT:(oc + 1) * TT], pA[:, 0:TT], AF.Identity,
                                                  bias=colsA[:, COL[bkey] + oc:COL[bkey] + oc + 1]), "qev", 1)

            @block.vector
            def _(vector, t=t):
                ctx.wait(vector, "qev")
                for c in range(NCH):
                    src_k = AP(E2, c * TT + 127, [[4096, 128], [128, 4], [-1, 128]])
                    i = vector.tensor_copy(E4[:, c * TT:(c + 1) * TT].rearrange("p (b f) -> p b f", b=4), src_k)
                ctx.inc(i, "qrev", 1)

            @block.scalar
            def _(scalar, t0=t0):
                ctx.wait(scalar, "qrev")
                ctx.inc(scalar.dma_start(out=chv(dram["qout"], t0, TT),
                                         in_=E1.ap().rearrange("p (c tk) -> p c tk", tk=TT)), "qst", 16)
                ctx.inc(scalar.dma_start(out=chv(dram["krev"], t0, TT),
                                         in_=E4.ap().rearrange("p (c tk) -> p c tk", tk=TT)), "qst", 16)
                ctx.inc(scalar.dma_start(out=chv(dram["vout"], t0, TT),
                                         in_=E3.ap().rearrange("p (c tk) -> p c tk", tk=TT)), "qst", 16)

    def stage_ffn1():
        block = blk[0]
        NQ = T // 1024
        for q in range(NQ):
            q0 = q * 1024

            @block.gpsimd
            def _(g, q=q, q0=q0):
                if q == 0:
                    ctx.wait(g, "Lfr")
                    load_cols(g, "gb1", "gact", n=NHC)
                    load_cols(g, "gln2g", "gact", n=NHC)
                    load_cols(g, "gln2b", "gact", n=NHC)
                    load_cols(g, "gb2", "gact")
                else:
                    ctx.wait(g, "fst")
                ctx.inc(g.dma_start(out=E5.ap().rearrange("p (c tk) -> p c tk", tk=1024),
                                    in_=chv(dram["st_g1"], q0, 1024)[:, 0:4, :]), "gact", 16)
                ctx.inc(g.dma_start(out=E6.ap().rearrange("p (c tk) -> p c tk", tk=1024),
                                    in_=chv(dram["st_g1"], q0, 1024)[:, 4:8, :]), "gact", 16)

            def hofs(ic):
                EH = E5 if ic < 4 else E6
                return EH, (ic % 4) * 1024

            for hc in range(NHC):
                @block.sync
                def _(sync, hc=hc, q=q):
                    par = hc % 2
                    if hc >= 2 or q > 0:
                        ctx.wait(sync, "pfree")
                    else:
                        ctx.wait(sync, "qmm")
                    src = dram["gwq1T"].ap()[:, hc * 128:(hc + 1) * 128].rearrange("(c p) o -> p c o", p=128)
                    ctx.inc(sync.dma_start(out=WA[:, par * 4096: par * 4096 + NCH * 128]
                                           .rearrange("p (c o) -> p c o", o=128), in_=src), "sld", 16)

                @block.tensor
                def _(tensor, hc=hc, q=q):
                    par = hc % 2
                    ctx.wait(tensor, "sld")
                    ctx.wait(tensor, "gact")
                    if hc >= 1 or q > 0:
                        ctx.wait(tensor, "fgel")
                    for ic in range(NCH):
                        EH, off = hofs(ic)
                        for hf in range(2):
                            i = tensor.matmul(pbig[:, hf * 512:(hf + 1) * 512],
                                              WA[:, par * 4096 + ic * 128: par * 4096 + (ic + 1) * 128],
                                              EH[:, off + hf * 512: off + (hf + 1) * 512],
                                              start=(ic == 0), stop=(ic == NCH - 1))
                    ctx.inc(i, "pfree", 1)

                @block.scalar
                def _(scalar, hc=hc):
                    ctx.wait(scalar, "pfree")
                    ctx.inc(scalar.activation(H8[:, hc * 1024:(hc + 1) * 1024], pbig[:, 0:1024], AF.Gelu,
                                              bias=colsA[:, COL["gb1"] + hc:COL["gb1"] + hc + 1]), "fgel", 1)

            for t2 in range(2):
                @block.tensor
                def _(tensor, t2=t2, q=q):
                    ctx.wait(tensor, "fgel")
                    if t2 != 0 or q > 0:
                        ctx.wait(tensor, "fap")
                    for hc in range(NHC):
                        i = tensor.matmul(prow[0:1, 0:512], onecb[:, 0:1],
                                          H8[:, hc * 1024 + t2 * 512: hc * 1024 + (t2 + 1) * 512],
                                          start=(hc == 0), stop=(hc == NHC - 1))
                    ctx.inc(i, "fsum1", 1)

                for hc in range(NHC):
                    @block.scalar
                    def _(scalar, t2=t2, hc=hc):
                        par = hc % 2
                        if hc >= 2:
                            ctx.wait(scalar, f"fsqm{par}")
                        if hc == 0:
                            ctx.wait(scalar, "fsum1")
                        ctx.inc(scalar.activation(E1[:, par * 512:(par + 1) * 512],
                                                  H8[:, hc * 1024 + t2 * 512: hc * 1024 + (t2 + 1) * 512],
                                                  AF.Square), f"fsq{par}", 1)

                    @block.tensor
                    def _(tensor, hc=hc):
                        par = hc % 2
                        ctx.wait(tensor, f"fsq{par}")
                        i = tensor.matmul(prow[0:1, 512:1024], onecb[:, 0:1], E1[:, par * 512:(par + 1) * 512],
                                          start=(hc == 0), stop=(hc == NHC - 1))
                        ctx.inc(i, f"fsqm{par}", 1)

                @block.vector
                def _(vector, t2=t2):
                    ctx.wait(vector, "fsqm0")
                    ctx.wait(vector, "fsqm1")
                    ctx.wait(vector, "fsum1")
                    vector.tensor_scalar_mul(rows[0:1, RM:RM + 512], prow[0:1, 0:512], 1.0 / HID)
                    vector.tensor_scalar_mul(rows[0:1, RV:RV + 512], prow[0:1, 512:1024], 1.0 / HID)
                    vector.tensor_mul(rows[0:1, RS:RS + 512], rows[0:1, RM:RM + 512], rows[0:1, RM:RM + 512])
                    vector.tensor_sub(rows[0:1, RV:RV + 512], rows[0:1, RV:RV + 512], rows[0:1, RS:RS + 512])
                    ctx.inc(vector.tensor_scalar_add(rows[0:1, RV:RV + 512], rows[0:1, RV:RV + 512], EPS),
                            "fvar", 1)

                @block.scalar
                def _(scalar, t2=t2):
                    ctx.wait(scalar, "fvar")
                    scalar.activation(rows[0:1, RS:RS + 512], rows[0:1, RV:RV + 512], AF.Ln)
                    ctx.inc(scalar.activation(rows[0:1, RSTD:RSTD + 512], rows[0:1, RS:RS + 512], AF.Exp,
                                              scale=-0.5), "frs", 1)

                @block.vector
                def _(vector, t2=t2):
                    ctx.wait(vector, "frs")
                    ctx.inc(vector.tensor_mul(rows[0:1, RMR:RMR + 512], rows[0:1, RM:RM + 512],
                                              rows[0:1, RSTD:RSTD + 512]), "fmr", 1)

                @block.tensor
                def _(tensor, t2=t2):
                    ctx.wait(tensor, "fmr")
                    tensor.matmul(pbig[:, 0:512], onerf[0:1, :], rows[0:1, RSTD:RSTD + 512], start=True, stop=True)
                    ctx.inc(tensor.matmul(pbig[:, 512:1024], onerf[0:1, :], rows[0:1, RMR:RMR + 512],
                                          start=True, stop=True), "fbc", 1)

                @block.vector
                def _(vector, t2=t2):
                    ctx.wait(vector, "fbc")
                    vector.tensor_copy(G32[:, 0:512], pbig[:, 0:512])
                    ctx.bar(vector, vector.tensor_copy(G32[:, 512:1024], pbig[:, 512:1024]))
                    for hc in range(NHC):
                        sl = H8[:, hc * 1024 + t2 * 512: hc * 1024 + (t2 + 1) * 512]
                        vector.tensor_mul(E2[:, 0:512], sl, G32[:, 0:512])
                        vector.tensor_sub(E2[:, 0:512], E2[:, 0:512], G32[:, 512:1024])
                        i = vector.tensor_scalar(sl, E2[:, 0:512],
                                                 colsA[:, COL["gln2g"] + hc:COL["gln2g"] + hc + 1],
                                                 colsA[:, COL["gln2b"] + hc:COL["gln2b"] + hc + 1],
                                                 ALU.mult, ALU.add)
                    ctx.inc(i, "fap", 1)

            for oc in range(NCH):
                @block.sync
                def _(sync, oc=oc, q=q):
                    par = oc % 2
                    if oc >= 2 or q > 0:
                        ctx.wait(sync, "pfree")
                    else:
                        ctx.wait(sync, "qmm")
                    src = dram["gwq2T"].ap()[:, oc * 128:(oc + 1) * 128].rearrange("(c p) o -> p c o", p=128)
                    ctx.inc(sync.dma_start(out=WB[:, par * 4096: par * 4096 + NHC * 128]
                                           .rearrange("p (c o) -> p c o", o=128), in_=src), "sld", 16)

                @block.tensor
                def _(tensor, oc=oc, q=q):
                    par = oc % 2
                    ctx.wait(tensor, "sld")
                    ctx.wait(tensor, "fap")
                    if oc >= 1 or q > 0:
                        ctx.wait(tensor, "fw2e")
                    for t2 in range(2):
                        dstp = pA if t2 == 0 else pB
                        for hc in range(NHC):
                            i = tensor.matmul(dstp[:, 0:512],
                                              WB[:, par * 4096 + hc * 128: par * 4096 + (hc + 1) * 128],
                                              H8[:, hc * 1024 + t2 * 512: hc * 1024 + (t2 + 1) * 512],
                                              start=(hc == 0), stop=(hc == NHC - 1))
                    ctx.inc(i, "pfree", 1)

                @block.scalar
                def _(scalar, oc=oc):
                    ctx.wait(scalar, "pfree")
                    scalar.activation(E2[:, oc * 512:(oc + 1) * 512], pA[:, 0:512], AF.Identity,
                                      bias=colsA[:, COL["gb2"] + oc:COL["gb2"] + oc + 1])
                    ctx.inc(scalar.activation(E3[:, oc * 512:(oc + 1) * 512], pB[:, 0:512], AF.Identity,
                                              bias=colsA[:, COL["gb2"] + oc:COL["gb2"] + oc + 1]), "fw2e", 1)

            @block.scalar
            def _(scalar, q0=q0):
                ctx.wait(scalar, "fw2e")
                ctx.inc(scalar.dma_start(out=chv(dram["ffn1out"], q0, 512),
                                         in_=E2.ap().rearrange("p (c tk) -> p c tk", tk=512)), "fst", 16)
                ctx.inc(scalar.dma_start(out=chv(dram["ffn1out"], q0 + 512, 512),
                                         in_=E3.ap().rearrange("p (c tk) -> p c tk", tk=512)), "fst", 16)

    with es, nc.Block() as block:
        blk[0] = block
        stage_init()
        ln_stage("xn", "xT", "st_xn")
        if stages >= 2:
            stage_attn()
        if stages >= 3:
            ln_stage("xl", "st_xn", "st_xln", affine=("fln1w", "fln1b"), prev=("atst",))
        if stages >= 4:
            stage_ffn_front()
        if stages >= 5:
            ln_stage("h2", "st_x1mid", "st_h", affine=("fln2w", "fln2b"), prev=("ffst",))
        if stages >= 6:
            stage_bids()
        if stages >= 7:
            stage_experts()
        if stages >= 8:
            ln_stage("x2", "st_x1", "st_xn2", prev=("xst",))
        if stages >= 9:
            stage_qkv()
        if stages >= 10:
            ln_stage("g1", "st_xn2", "st_g1", affine=("gln1g", "gln1b"), prev=("qst",))
            stage_ffn1()

        @block.sync
        def _(sync):
            ctx.wait(sync, "Lfr")
            if stages >= 2:
                ctx.wait(sync, "atst")
            if stages >= 3:
                ctx.wait(sync, "Lfr")
            if stages >= 4:
                ctx.wait(sync, "ffst")
            if stages >= 5:
                ctx.wait(sync, "Lfr")
            if stages >= 6:
                ctx.wait(sync, "bmst")
            if stages >= 7:
                ctx.wait(sync, "xst")
            if stages >= 8:
                ctx.wait(sync, "Lfr")
            if stages >= 9:
                ctx.wait(sync, "qst")
            if stages >= 10:
                ctx.wait(sync, "Lfr")
                ctx.wait(sync, "fst")
            if stages < 9:
                ctx.inc(sync.dma_start(out=dram["krev"].ap(), in_=dram["st_xn"].ap()), "finst", 16)
            if 2 <= stages < 9:
                ctx.inc(sync.dma_start(out=dram["vout"].ap(), in_=dram["st_attn"].ap()), "finst", 16)
            if 3 <= stages < 9:
                ctx.inc(sync.dma_start(out=dram["qout"].ap(), in_=dram["st_xln"].ap()), "finst", 16)
            if 4 <= stages < 7:
                ctx.inc(sync.dma_start(out=dram["x1out"].ap(), in_=dram["st_x1mid"].ap()), "finst", 16)
                ctx.inc(sync.dma_start(out=dram["ffn1out"].ap(), in_=dram["st_state"].ap()), "finst", 16)
            if 6 <= stages < 9:
                ctx.inc(sync.dma_start(out=dram["ffn1out"].ap()[0:8, :], in_=dram["st_mrows"].ap()), "finst", 16)
            if stages == 9:
                ctx.inc(sync.dma_start(out=dram["ffn1out"].ap(), in_=dram["st_xn2"].ap()), "finst", 16)

            ctx.wait(sync, "finst")
    return nc



DC = 512
L2T = 8192
NF = L2T // 128
NS = 17
KW = 2304
SKW = NS * 128
CG = 32
NG = DC // CG


def build_l2(debug=False):
    nc = bass.Bass("TRN2", target_bir_lowering=False)
    dram = {}

    def din(name, shape, dt=BF16):
        dram[name] = nc.dram_tensor(name, shape, dt, kind="ExternalInput")
        return dram[name]

    din("kin", [DC, L2T]); din("vin", [DC, L2T])
    din("kerp1", [DC, KW], F32); din("kerp2", [DC, KW], F32)
    din("kdc", [DC, 1], F32); din("ddc", [DC, 1], F32)
    yout = nc.dram_tensor("yout", [DC, L2T], BF16, kind="ExternalOutput")
    kb = [nc.dram_tensor(f"kb{k}", [DC, KW], BF16, kind="Internal") for k in range(2)]
    kvrevfm = nc.dram_tensor("kvrevfm", [DC, L2T], BF16,
                             kind="ExternalOutput" if debug else "Internal")

    es = contextlib.ExitStack()
    ctx = Ctx(nc, es)
    K32 = sbuf(ctx, "K32", [128, KW], F32)
    KB16 = sbuf(ctx, "KB16", [128, KW], BF16)
    kcol = sbuf(ctx, "kcol", [128, 1], F32)
    HS = [sbuf(ctx, f"HS{i}", [128, SKW], BF16) for i in range(2)]
    XGT = [sbuf(ctx, f"XGT{i}", [128, CG * 64], BF16) for i in range(2)]
    XG = [sbuf(ctx, f"XG{i}", [128, CG * 80], BF16) for i in range(2)]
    VGB = [sbuf(ctx, f"VGB{i}", [64, CG * 128], BF16) for i in range(2)]
    GB = [sbuf(ctx, f"GB{i}", [64, CG * 128], BF16) for i in range(2)]
    KVS = sbuf(ctx, "KVS", [64, 256], BF16)
    KVF = sbuf(ctx, "KVF", [64, 256], BF16)
    pp = [psum(ctx, f"pp{i}", [128, 512], F32) for i in range(2)]

    for nm in ("cmm", "cev", "sklde", "skldo", "xld", "xw", "vld", "gst"):
        ctx.sem(nm); ctx.cnt.setdefault(nm, 0)

    def conv_pass(cv, block):
        kbm = kb[cv]
        xsrc = dram["kin"] if cv == 0 else kvrevfm
        b_cmm = ctx.cnt["cmm"]; b_cev = ctx.cnt["cev"]
        b_se = ctx.cnt["sklde"]; b_so = ctx.cnt["skldo"]; b_xld = ctx.cnt["xld"]
        b_xw = ctx.cnt["xw"]; b_vld = ctx.cnt["vld"]; b_gst = ctx.cnt["gst"]

        def emit_xload(gi):
            @block.sync
            def _(sync, gi=gi):
                if gi == 0:
                    ctx.wait(sync, "pst")
                    if cv == 1:
                        ctx.wait(sync, "gst")
                if gi >= 2:
                    sync.wait_ge(ctx.sem("xw"), b_xw + gi - 1)
                d0 = gi * CG
                src = xsrc.ap()[d0:d0 + CG, :].rearrange("g (F p) -> (g F) p", p=128)
                ctx.inc(sync.dma_start(out=XGT[gi % 2][:], in_=src, transpose=True), "xld", 16)

        def emit_vload(gi):
            @block.gpsimd
            def _(g, gi=gi):
                if gi >= 2:
                    g.wait_ge(ctx.sem("cev"), b_cev + (gi - 1) * CG)
                d0 = gi * CG
                ctx.inc(g.dma_start(
                    out=VGB[gi % 2].ap().rearrange("F (g j) -> F g j", j=128),
                    in_=dram["vin"].ap()[d0:d0 + CG, :].rearrange("g (F j) -> F g j", j=128)), "vld", 16)

        def emit_wcopy(gi):
            @block.vector
            def _(vector, gi=gi):
                vector.wait_ge(ctx.sem("xld"), b_xld + 16 * (gi + 1))
                if gi >= 2:
                    vector.wait_ge(ctx.sem("cmm"), b_cmm + (gi - 1) * CG)
                ctx.inc(vector.tensor_copy(
                    XG[gi % 2].ap().rearrange("p (g w) -> p g w", w=80)[:, :, 16:80],
                    XGT[gi % 2].ap().rearrange("p (g F) -> p g F", F=64)), "xw", 1)

        emit_xload(0)
        if cv == 0:
            emit_vload(0)
        emit_xload(1)
        if cv == 0:
            emit_vload(1)
        emit_wcopy(0)

        for d in range(DC):
            gi = d // CG
            par = d % 2
            if d % CG == 0 and gi > 0:
                if gi + 1 < NG:
                    emit_xload(gi + 1)
                    if cv == 0:
                        emit_vload(gi + 1)
                emit_wcopy(gi)

            def skew_body(eng, d=d):
                if d < 2:
                    ctx.wait(eng, "pst")
                    if cv == 1:
                        eng.wait_ge(ctx.sem("cmm"), b_cmm)
                else:
                    eng.wait_ge(ctx.sem("cmm"), b_cmm + d - 1)
                ctx.inc(eng.dma_start(
                    out=HS[d % 2].ap().rearrange("p (s j) -> p s j", j=128),
                    in_=AP(kbm, d * KW, [[1, 128], [128, NS], [1, 128]])),
                    "sklde" if d % 2 == 0 else "skldo", 16)

            if par == 0:
                @block.sync
                def _(sync, d=d):
                    skew_body(sync, d)
            else:
                @block.scalar
                def _(scalar, d=d):
                    skew_body(scalar, d)

            @block.tensor
            def _(tensor, d=d, gi=gi):
                if d % 2 == 0:
                    tensor.wait_ge(ctx.sem("sklde"), b_se + 16 * (d // 2 + 1))
                else:
                    tensor.wait_ge(ctx.sem("skldo"), b_so + 16 * ((d - 1) // 2 + 1))
                tensor.wait_ge(ctx.sem("xw"), b_xw + gi + 1)
                if d >= 2:
                    tensor.wait_ge(ctx.sem("cev"), b_cev + d - 1)
                elif cv == 1:
                    ctx.wait(tensor, "cev")
                w0 = (d % CG) * 80
                for ss in range(NS):
                    i = tensor.matmul(pp[d % 2][0:64, 0:128],
                                      XG[gi % 2][:, w0 + 16 - ss: w0 + 80 - ss],
                                      HS[d % 2][:, ss * 128:(ss + 1) * 128],
                                      start=(ss == 0), stop=(ss == NS - 1))
                ctx.inc(i, "cmm", 1)

            @block.vector
            def _(vector, d=d, gi=gi, par=par):
                vector.wait_ge(ctx.sem("cmm"), b_cmm + d + 1)
                if d % CG == 0 and gi >= 2:
                    vector.wait_ge(ctx.sem("gst"), b_gst + 16 * (gi - 1))
                elif cv == 1 and d % CG == 0:
                    vector.wait_ge(ctx.sem("gst"), b_gst)
                o0 = (d % CG) * 128
                if cv == 0:
                    vector.wait_ge(ctx.sem("vld"), b_vld + 16 * (gi + 1))
                    ctx.bar(vector, vector.tensor_copy(KVF[:, par * 128:(par + 1) * 128],
                                                       pp[d % 2][0:64, 0:128]))
                    ctx.bar(vector, vector.tensor_mul(KVS[:, par * 128:(par + 1) * 128],
                                                      KVF[:, par * 128:(par + 1) * 128],
                                                      VGB[gi % 2][:, o0:o0 + 128]))
                    src = AP(KVS, par * 128 + 127, [[256, 64], [-1, 128]])
                    ctx.inc(vector.tensor_copy(GB[gi % 2][:, o0:o0 + 128], src), "cev", 1)
                else:
                    ctx.inc(vector.tensor_copy(GB[gi % 2][:, o0:o0 + 128], pp[d % 2][0:64, 0:128]),
                            "cev", 1)

            if d % CG == CG - 1:
                @block.gpsimd
                def _(g, gi=gi):
                    g.wait_ge(ctx.sem("cev"), b_cev + (gi + 1) * CG)
                    d0 = gi * CG
                    dst = (kvrevfm if cv == 0 else yout).ap()[d0:d0 + CG, :]
                    ctx.inc(g.dma_start(
                        out=dst.rearrange("g (F j) -> F g j", j=128),
                        in_=GB[gi % 2].ap().rearrange("F (g j) -> F g j", j=128)), "gst", 16)

    with es, nc.Block() as block:
        @block.vector
        def _(vector):
            vector.memset(XG[0][:], 0.0)
            ctx.inc(vector.memset(XG[1][:], 0.0), "xz", 1)

        for kk in range(2):
            for g4 in range(DC // 128):
                it = kk * 4 + g4

                @block.gpsimd
                def _(g, kk=kk, g4=g4, it=it):
                    if it > 0:
                        ctx.wait(g, "pc")
                    src = dram["kerp1"] if kk == 0 else dram["kerp2"]
                    col = dram["kdc"] if kk == 0 else dram["ddc"]
                    ctx.inc(g.dma_start(out=K32[:], in_=src.ap()[g4 * 128:(g4 + 1) * 128, :]), "pld", 16)
                    ctx.inc(g.dma_start(out=kcol[:], in_=col.ap()[g4 * 128:(g4 + 1) * 128, :]), "pld", 16)

                @block.vector
                def _(vector, it=it):
                    ctx.wait(vector, "pld")
                    if it > 0:
                        ctx.wait(vector, "pst")
                    ctx.bar(vector, vector.tensor_add(K32[:, 127:128], K32[:, 127:128], kcol[:, 0:1]))
                    ctx.inc(vector.tensor_copy(KB16[:], K32[:]), "pc", 1)

                @block.gpsimd
                def _(g, kk=kk, g4=g4):
                    ctx.wait(g, "pc")
                    ctx.inc(g.dma_start(out=kb[kk].ap()[g4 * 128:(g4 + 1) * 128, :], in_=KB16[:]), "pst", 16)

        conv_pass(0, block)
        conv_pass(1, block)

        @block.sync
        def _(sync):
            ctx.wait(sync, "gst")
    return nc



def build_l3():
    nc = bass.Bass("TRN2", target_bir_lowering=False)
    dram = {}

    def din(name, shape, dt=BF16):
        dram[name] = nc.dram_tensor(name, shape, dt, kind="ExternalInput")
        return dram[name]

    din("yin", [D, T]); din("qin", [D, T]); din("x1in", [D, T]); din("f1in", [D, T])
    din("qWoT", [D, D], F32)
    din("qbo", [128, NCH], F32)
    outT = nc.dram_tensor("outT", [D, T], F32, kind="ExternalOutput")

    es = contextlib.ExitStack()
    ctx = Ctx(nc, es)
    EY = [sbuf(ctx, f"EY{i}", [128, NCH * TT], BF16) for i in range(2)]
    EQ = [sbuf(ctx, f"EQ{i}", [128, NCH * TT], BF16) for i in range(2)]
    EX = [sbuf(ctx, f"EX{i}", [128, NCH * TT], BF16) for i in range(2)]
    EF = [sbuf(ctx, f"EF{i}", [128, NCH * TT], BF16) for i in range(2)]
    YQ = [sbuf(ctx, f"YQ{i}", [128, NCH * TT], BF16) for i in range(2)]
    WA = sbuf(ctx, "WA", [128, NCH * D], BF16)
    colsA = sbuf(ctx, "colsA", [128, NCH], F32)
    FO = sbuf(ctx, "FO", [128, 2 * TT], F32)
    FT = sbuf(ctx, "FT", [128, 2 * TT], F32)
    OUTB = [sbuf(ctx, f"OUTB{i}", [128, NCH * TT], F32) for i in range(2)]
    pA = psum(ctx, "pA", [128, TT], F32)
    pB = psum(ctx, "pB", [128, TT], F32)

    def w_tile(w, ic, oc):
        return w[:, ic * D + oc * 128: ic * D + (oc + 1) * 128]

    with es, nc.Block() as block:
        @block.gpsimd
        def _(g):
            src = dram["qWoT"].ap().rearrange("(c p) o -> p c o", p=128)
            ctx.inc(g.dma_start(out=WA[:, 0:NCH * D].rearrange("p (c o) -> p c o", o=D), in_=src),
                    "gwl", 16)
            ctx.inc(g.dma_start(out=colsA[:, 0:NCH], in_=dram["qbo"].ap()), "gwl", 16)

        for t in range(NT):
            t0 = t * TT
            par = t % 2

            @block.gpsimd
            def _(g, t=t, t0=t0, par=par):
                if t >= 2:
                    g.wait_ge(ctx.sem("l3st"), 16 * (t - 1))
                for nm, dst in (("yin", EY), ("qin", EQ), ("x1in", EX), ("f1in", EF)):
                    ctx.inc(g.dma_start(out=dst[par].ap().rearrange("p (c tk) -> p c tk", tk=TT),
                                        in_=chv(dram[nm], t0, TT)), "l3ld", 16)

            @block.vector
            def _(vector, t=t, par=par):
                ctx.wait(vector, "l3ld")
                if t >= 2:
                    vector.wait_ge(ctx.sem("l3mm"), 8 * (t - 1))
                ctx.inc(vector.tensor_mul(YQ[par][:, :], EY[par][:, :], EQ[par][:, :]), "l3x", 1)

            for oc in range(NCH):
                @block.tensor
                def _(tensor, t=t, oc=oc, par=par):
                    ctx.wait(tensor, "l3x")
                    if t == 0 and oc == 0:
                        ctx.wait(tensor, "gwl")
                    if not (t == 0 and oc < 2):
                        ctx.wait(tensor, "l3ev")
                    dstp = pA if oc % 2 == 0 else pB
                    for ic in range(NCH):
                        i = tensor.matmul(dstp[:, 0:TT], w_tile(WA, ic, oc),
                                          YQ[par][:, ic * TT:(ic + 1) * TT],
                                          start=(ic == 0), stop=(ic == NCH - 1))
                    ctx.inc(i, "l3mm", 1)

                @block.scalar
                def _(scalar, t=t, oc=oc):
                    ctx.wait(scalar, "l3mm")
                    if not (t == 0 and oc < 2):
                        ctx.wait(scalar, "l3ev")
                    srcp = pA if oc % 2 == 0 else pB
                    ctx.inc(scalar.activation(FO[:, (oc % 2) * TT:(oc % 2 + 1) * TT], srcp[:, 0:TT],
                                              AF.Identity, bias=colsA[:, oc:oc + 1]), "l3sc", 1)

                @block.vector
                def _(vector, t=t, oc=oc, par=par):
                    ctx.wait(vector, "l3sc")
                    if oc == 0 and t >= 2:
                        vector.wait_ge(ctx.sem("l3st"), 16 * (t - 1))
                    o2 = (oc % 2) * TT
                    vector.tensor_add(FT[:, o2:o2 + TT], EX[par][:, oc * TT:(oc + 1) * TT],
                                      EF[par][:, oc * TT:(oc + 1) * TT])
                    ctx.inc(vector.tensor_add(OUTB[par][:, oc * TT:(oc + 1) * TT],
                                              FO[:, o2:o2 + TT], FT[:, o2:o2 + TT]), "l3ev", 1)

            @block.scalar
            def _(scalar, t=t, t0=t0, par=par):
                ctx.wait(scalar, "l3ev")
                ctx.inc(scalar.dma_start(out=chv(outT, t0, TT),
                                         in_=OUTB[par].ap().rearrange("p (c tk) -> p c tk", tk=TT)),
                        "l3st", 16)

        @block.sync
        def _(sync):
            ctx.wait(sync, "l3st")
    return nc



def _prep(inputs):
    f32 = np.float32
    P = {}
    P["w1t"] = np.ascontiguousarray(inputs["L1_ffn_w1"].T)
    P["w2t"] = np.ascontiguousarray(inputs["L1_ffn_w2"].T)

    com = {
        "aWq": inputs["L0_attn_Wq"], "aWk": inputs["L0_attn_Wk"],
        "aWv": inputs["L0_attn_Wv"], "aWo": inputs["L0_attn_Wo"],
        "abo": inputs["L0_attn_bo"].reshape(8, 128).T,
        "fln1w": inputs["L0_ffn_ln1_w"].reshape(8, 128).T,
        "fln1b": inputs["L0_ffn_ln1_b"].reshape(8, 128).T,
        "fln2w": inputs["L0_ffn_ln2_w"].reshape(8, 128).T,
        "fln2b": inputs["L0_ffn_ln2_b"].reshape(8, 128).T,
        "fWr": inputs["L0_ffn_Wr"], "fWv": inputs["L0_ffn_Wv"],
        "fWo": inputs["L0_ffn_Wo"], "fWs": inputs["L0_ffn_Ws"],
        "fW1": inputs["L0_ffn_W1"], "fW2": inputs["L0_ffn_W2"], "fW3": inputs["L0_ffn_W3"],
        "fK": inputs["L0_ffn_K_rwkv"], "fV": inputs["L0_ffn_V_rwkv"],
        "fconfW": np.concatenate(
            [inputs["L0_ffn_conf_rwkv"].T, inputs["L0_ffn_conf_trans"][:, None],
             inputs["L0_ffn_w_diff"][:, None], inputs["L0_ffn_W_aff"],
             np.zeros((D, 1), f32)], axis=1),
        "fcs": np.concatenate([inputs["L0_ffn_capital_shares"], np.zeros(5, f32)])[None, :],
        "qWqT": np.ascontiguousarray(inputs["L1_attn_Wq"].T),
        "qWkT": np.ascontiguousarray(inputs["L1_attn_Wk"].T),
        "qWvT": np.ascontiguousarray(inputs["L1_attn_Wv"].T),
        "qbq": inputs["L1_attn_bq"].reshape(8, 128).T,
        "qbk": inputs["L1_attn_bk"].reshape(8, 128).T,
        "qbv": inputs["L1_attn_bv"].reshape(8, 128).T,
        "gb1": inputs["L1_ffn_b1"].reshape(32, 128).T,
        "gln1g": inputs["L1_ffn_ln1_g"].reshape(8, 128).T,
        "gln1b": inputs["L1_ffn_ln1_b"].reshape(8, 128).T,
        "gln2g": inputs["L1_ffn_ln2_g"].reshape(32, 128).T,
        "gln2b": inputs["L1_ffn_ln2_b"].reshape(32, 128).T,
        "gb2": inputs["L1_ffn_b2"].reshape(8, 128).T,
    }
    P["com"] = {k: np.ascontiguousarray(v, dtype=f32) for k, v in com.items()}

    kerp1 = np.zeros((D, KW), f32); kerp1[:, 127:127 + 2048] = inputs["L1_attn_k_kernel"]
    kerp2 = np.zeros((D, KW), f32); kerp2[:, 127:127 + 2048] = inputs["L1_attn_ssm_kernel"]
    P["kerp1"] = kerp1; P["kerp2"] = kerp2
    P["kdc"] = np.ascontiguousarray(inputs["L1_attn_k_D"][:, None], dtype=f32)
    P["ddc"] = np.ascontiguousarray(inputs["L1_attn_D"][:, None], dtype=f32)
    P["qWoT"] = np.ascontiguousarray(inputs["L1_attn_Wo"].T, dtype=f32)
    P["qbo"] = np.ascontiguousarray(inputs["L1_attn_bo"].reshape(8, 128).T, dtype=f32)
    return P


def kernel(**inputs):
    P = _prep(inputs)
    x = inputs["L0_attn_query"]

    nc0 = build_l0()
    in0 = [{"w1t": P["w1t"], "w2t": P["w2t"]} for _ in range(NCORE)]
    r0 = run_bass_kernel_spmd(nc0, in0, core_ids=list(range(NCORE)))
    gwq1T = np.ascontiguousarray(np.concatenate([r0.results[c]["wq1s"] for c in range(NCORE)], axis=0))
    gwq2T = np.ascontiguousarray(np.concatenate([r0.results[c]["wq2s"] for c in range(NCORE)], axis=0))

    nc1 = build_l1()
    feeds1 = []
    for c in range(NCORE):
        b, h = c // 2, c % 2
        f = dict(P["com"])
        f["xT"] = np.ascontiguousarray(x[b, h * T:(h + 1) * T, :].T)
        f["gwq1T"] = gwq1T; f["gwq2T"] = gwq2T
        feeds1.append(f)
    r1 = run_bass_kernel_spmd(nc1, feeds1, core_ids=list(range(NCORE)))
    R1 = [r1.results[c] for c in range(NCORE)]

    nc2 = build_l2()
    feeds2 = []
    for c in range(NCORE):
        b, h = c // 2, c % 2
        d0 = h * DC
        f = {
            "kin": np.ascontiguousarray(np.concatenate(
                [R1[2 * b]["krev"][d0:d0 + DC, :], R1[2 * b + 1]["krev"][d0:d0 + DC, :]], axis=1)),
            "vin": np.ascontiguousarray(np.concatenate(
                [R1[2 * b]["vout"][d0:d0 + DC, :], R1[2 * b + 1]["vout"][d0:d0 + DC, :]], axis=1)),
            "kerp1": P["kerp1"][d0:d0 + DC], "kerp2": P["kerp2"][d0:d0 + DC],
            "kdc": P["kdc"][d0:d0 + DC], "ddc": P["ddc"][d0:d0 + DC],
        }
        feeds2.append({k: np.ascontiguousarray(v) for k, v in f.items()})
    r2 = run_bass_kernel_spmd(nc2, feeds2, core_ids=list(range(NCORE)))
    R2 = [r2.results[c] for c in range(NCORE)]

    nc3 = build_l3()
    feeds3 = []
    for c in range(NCORE):
        b, h = c // 2, c % 2
        yfull = np.concatenate([R2[2 * b]["yout"], R2[2 * b + 1]["yout"]], axis=0)
        f = {
            "yin": np.ascontiguousarray(yfull[:, h * T:(h + 1) * T]),
            "qin": R1[c]["qout"], "x1in": R1[c]["x1out"], "f1in": R1[c]["ffn1out"],
            "qWoT": P["qWoT"], "qbo": P["qbo"],
        }
        feeds3.append(f)
    r3 = run_bass_kernel_spmd(nc3, feeds3, core_ids=list(range(NCORE)))

    out = np.empty((4, 2 * T, D), np.float32)
    for c in range(NCORE):
        b, h = c // 2, c % 2
        out[b, h * T:(h + 1) * T, :] = r3.results[c]["outT"].T
    return out
